# version 01 (final)
# baseline (speedup 1.0000x reference)
import numpy as np
from contextlib import ExitStack
import concourse.bass as bass
import concourse.mybir as mybir
from concourse.bass_utils import run_bass_kernel_spmd

F32 = mybir.dt.float32
BF16 = mybir.dt.bfloat16
AF = mybir.ActivationFunctionType
ALU = mybir.AluOpType
AX = mybir.AxisListType

ENGS = ("sync", "tensor", "vector", "scalar", "gpsimd")

L = 4096
D = 1024
DIN = 2048
NH = 32
HD = 64
DS = 128
CH = 256
NCH = L // CH


class P:

    def __init__(self, nc, sem):
        self.nc = nc
        self.sem = sem
        self.v = 0
        self.q = {e: [] for e in ENGS}

    def _add(self, eng, fn, inc, wait):
        wv = self.v if wait else -1
        self.q[eng].append((wv, fn, inc))
        self.v += inc

    def dma(self, dst, src, eng="sync", wait=True, slow=False):
        def fn(E, d=dst, s=src, sl=slow):
            if sl:
                return E.dma_start(out=d, in_=s, allow_slow_non_contiguous=True)
            return E.dma_start(out=d, in_=s)
        self._add(eng, fn, 16, wait)

    def mm(self, out, lhsT, rhs, start=True, stop=True, wait=True, inc=True):
        self._add("tensor",
                  lambda E, o=out, a=lhsT, b=rhs, st=start, sp=stop:
                  E.matmul(o, a, b, start=st, stop=sp),
                  1 if inc else 0, wait)

    def trans(self, out, in_, ident, wait=True, inc=True):
        self._add("tensor",
                  lambda E, o=out, i=in_, d=ident: E.transpose(o, i, d),
                  1 if inc else 0, wait)

    def act(self, out, in_, func, bias=None, scale=None):
        def fn(E, o=out, i=in_, f=func, b=bias, s=scale):
            kw = {}
            if b is not None:
                kw["bias"] = b
            if s is not None:
                kw["scale"] = s
            return E.activation(out=o, in_=i, func=f, **kw)
        self._add("scalar", fn, 1, True)

    def cp(self, out, in_):
        self._add("vector", lambda E, o=out, i=in_: E.tensor_copy(out=o, in_=i), 1, True)

    def tt(self, out, a, b, op):
        self._add("vector", lambda E, o=out, x=a, y=b, p=op:
                  E.tensor_tensor(out=o, in0=x, in1=y, op=p), 1, True)

    def ts(self, out, in0, s1, op0, s2=None, op1=None):
        def fn(E, o=out, i=in0, a=s1, b=s2, p0=op0, p1=op1):
            if p1 is None:
                return E.tensor_scalar(out=o, in0=i, scalar1=a, scalar2=None, op0=p0)
            return E.tensor_scalar(out=o, in0=i, scalar1=a, scalar2=b, op0=p0, op1=p1)
        self._add("vector", fn, 1, True)

    def stt(self, out, in0, scalar, in1, op0, op1):
        self._add("vector", lambda E, o=out, i=in0, s=scalar, j=in1, p0=op0, p1=op1:
                  E.scalar_tensor_tensor(out=o, in0=i, scalar=s, in1=j, op0=p0, op1=p1),
                  1, True)

    def red(self, out, in_, op):
        self._add("vector", lambda E, o=out, i=in_, p=op:
                  E.tensor_reduce(o, i, AX.X, p), 1, True)

    def memset(self, ap, val):
        self._add("vector", lambda E, a=ap, v=val: E.memset(a, v), 1, True)

    def recip(self, out, in_):
        self._add("vector", lambda E, o=out, i=in_: E.reciprocal(o, i), 1, True)

    def finalize(self, block):
        for eng in ENGS:
            ops = self.q[eng]
            if not ops and eng != "gpsimd":
                continue
            dec = getattr(block, eng)

            def make(body_ops, is_last):
                def body(E):
                    last_wait = -1
                    for wv, fn, inc in body_ops:
                        if wv >= 0 and wv > last_wait:
                            E.wait_ge(self.sem, wv)
                            last_wait = wv
                        inst = fn(E)
                        if inc:
                            inst.then_inc(self.sem, inc)
                    if is_last:
                        E.wait_ge(self.sem, self.v)
                return body
            dec(make(ops, eng == "gpsimd"))


def new_prog(name):
    nc = bass.Bass("TRN2", target_bir_lowering=False)
    return nc


def build_A1():
    nc = new_prog("A1")
    u = nc.dram_tensor("u", [L, D], F32, kind="ExternalInput").ap()
    w_in = nc.dram_tensor("w_in", [D, 4384], F32, kind="ExternalInput").ap()
    w_out = nc.dram_tensor("w_out", [DIN, D], F32, kind="ExternalInput").ap()
    identb_h = nc.dram_tensor("identb", [128, 128], F32, kind="ExternalInput").ap()
    zf = nc.dram_tensor("zf", [DIN, L], F32, kind="ExternalOutput").ap()
    xbcp = nc.dram_tensor("xbcp", [2304, L], F32, kind="ExternalOutput").ap()
    dtraw = nc.dram_tensor("dtraw", [NH, L], F32, kind="ExternalOutput").ap()
    woutb = nc.dram_tensor("woutb", [DIN, D], BF16, kind="ExternalOutput").ap()

    es = ExitStack()
    uT = es.enter_context(nc.sbuf_tensor("uT", [128, 8, L], BF16))
    uin = es.enter_context(nc.sbuf_tensor("uin", [128, 4, 1024], F32))
    ubf = es.enter_context(nc.sbuf_tensor("ubf", [128, 4, 1024], BF16))
    idb = es.enter_context(nc.sbuf_tensor("identb_s", [128, 128], BF16))
    idstage = es.enter_context(nc.sbuf_tensor("idstage", [128, 128], F32))
    wstage = es.enter_context(nc.sbuf_tensor("wstage", [128, 8, 128], F32))
    wbfm = es.enter_context(nc.sbuf_tensor("wbfm", [128, 8, 128], BF16))
    evac = es.enter_context(nc.sbuf_tensor("evac", [128, 512], F32))
    wos = es.enter_context(nc.sbuf_tensor("wos", [128, 1024], F32))
    wob = es.enter_context(nc.sbuf_tensor("wob", [128, 1024], BF16))
    pz = es.enter_context(nc.psum_tensor("pz", [128, 512], F32))
    ptr = es.enter_context(nc.psum_tensor("ptr", [128, 128], BF16))
    g = es.enter_context(nc.semaphore("g"))
    block = es.enter_context(nc.Block())
    p = P(nc, g)
    p.dma(idstage[:, :], identb_h[:, :])
    p.cp(idb[:, :], idstage[:, :])
    for t in range(16):
        p.dma(wos[:, :], w_out[t * 128:(t + 1) * 128, :])
        p.cp(wob[:, :], wos[:, :])
        p.dma(woutb[t * 128:(t + 1) * 128, :], wob[:, :], eng="gpsimd")
    ur = u.rearrange("(a p) d -> p a d", p=128)
    for nt in range(8):
        p.dma(uin[:, :, :], ur[:, nt * 4:(nt + 1) * 4, :])
        p.cp(ubf[:, :, :], uin[:, :, :])
        for j in range(4):
            for k in range(8):
                p.trans(ptr[:, :], ubf[:, j, k * 128:(k + 1) * 128], idb[:, :])
                p.cp(uT[:, k, nt * 512 + j * 128: nt * 512 + (j + 1) * 128], ptr[:, :])
    wr = w_in.rearrange("(k p) f -> p k f", p=128)
    for m in range(35):
        mrows = 128 if m < 34 else 32
        p.dma(wstage[:, :, 0:mrows], wr[:, :, m * 128: m * 128 + mrows])
        p.cp(wbfm[:, :, 0:mrows], wstage[:, :, 0:mrows])
        for nt in range(8):
            for k in range(8):
                p.mm(pz[0:mrows, :], wbfm[:, k, 0:mrows],
                     uT[:, k, nt * 512:(nt + 1) * 512],
                     start=(k == 0), stop=(k == 7), wait=(k == 0), inc=(k == 7))
            c0, c1 = nt * 512, (nt + 1) * 512
            if m < 16:
                p.act(evac[:, :], pz[:, :], AF.Silu)
                p.dma(zf[m * 128:(m + 1) * 128, c0:c1], evac[:, :], eng="gpsimd")
            elif m < 34:
                p.cp(evac[:, :], pz[:, :])
                p.dma(xbcp[(m - 16) * 128:(m - 15) * 128, c0:c1], evac[:, :], eng="gpsimd")
            else:
                p.cp(evac[0:32, :], pz[0:32, :])
                p.dma(dtraw[:, c0:c1], evac[0:32, :], eng="gpsimd")
    p.finalize(block)
    es.close()
    return nc


def build_A2():
    nc = new_prog("A2")
    u = nc.dram_tensor("u", [L, D], F32, kind="ExternalInput").ap()
    zf = nc.dram_tensor("zf", [DIN, L], F32, kind="ExternalInput").ap()
    xbcp = nc.dram_tensor("xbcp", [2304, L], F32, kind="ExternalInput").ap()
    dtraw = nc.dram_tensor("dtraw", [NH, L], F32, kind="ExternalInput").ap()
    woutb_h = nc.dram_tensor("woutb", [DIN, D], BF16, kind="ExternalInput").ap()
    conv_w = nc.dram_tensor("conv_w", [2304, 4], F32, kind="ExternalInput").ap()
    conv_b = nc.dram_tensor("conv_b", [2304], F32, kind="ExternalInput").ap()
    dt_bias = nc.dram_tensor("dt_bias", [NH], F32, kind="ExternalInput").ap()
    a_log = nc.dram_tensor("a_log", [NH], F32, kind="ExternalInput").ap()
    norm_w = nc.dram_tensor("norm_w", [DIN], F32, kind="ExternalInput").ap()
    identf_h = nc.dram_tensor("identf", [128, 128], F32, kind="ExternalInput").ap()
    tri_h = nc.dram_tensor("tri", [CH, CH], F32, kind="ExternalInput").ap()
    sel_h = nc.dram_tensor("sel", [NH, NH * 128], F32, kind="ExternalInput").ap()
    e2_h = nc.dram_tensor("e2", [NH, DIN], F32, kind="ExternalInput").ap()
    x1 = nc.dram_tensor("x1", [D, L], F32, kind="ExternalOutput").ap()

    es = ExitStack()
    idf = es.enter_context(nc.sbuf_tensor("idf", [128, 128], F32))
    tri = es.enter_context(nc.sbuf_tensor("tri_s", [128, 2, CH], F32))
    sel = es.enter_context(nc.sbuf_tensor("sel_s", [NH, NH * 128], F32))
    e2 = es.enter_context(nc.sbuf_tensor("e2_s", [NH, DIN], F32))
    onesf = es.enter_context(nc.sbuf_tensor("onesf", [128, 128], F32))
    cbias = es.enter_context(nc.sbuf_tensor("cbias", [128, 2], F32))
    convw = es.enter_context(nc.sbuf_tensor("convw", [128, 18, 4], F32))
    convb = es.enter_context(nc.sbuf_tensor("convb", [128, 18], F32))
    nw = es.enter_context(nc.sbuf_tensor("nw", [128, 16], F32))
    dtb = es.enter_context(nc.sbuf_tensor("dtb", [NH, 1], F32))
    aneg = es.enter_context(nc.sbuf_tensor("aneg", [NH, 1], F32))
    dtall = es.enter_context(nc.sbuf_tensor("dtall", [NH, L], F32))
    spa = es.enter_context(nc.sbuf_tensor("spa", [NH, 1024], F32))
    spb = es.enter_context(nc.sbuf_tensor("spb", [NH, 1024], F32))
    xpre = es.enter_context(nc.sbuf_tensor("xpre", [128, 18, 260], F32))
    xconv = es.enter_context(nc.sbuf_tensor("xconv", [128, 18, CH], F32))
    xdt = es.enter_context(nc.sbuf_tensor("xdt", [128, 2, DIN], BF16))
    xdts = es.enter_context(nc.sbuf_tensor("xdts", [128, 2, DIN], BF16))
    adtc = es.enter_context(nc.sbuf_tensor("adtc", [NH, CH], F32))
    adtT = es.enter_context(nc.sbuf_tensor("adtT", [128, 2, NH], F32))
    acs = es.enter_context(nc.sbuf_tensor("acs", [NH, CH], F32))
    acst = es.enter_context(nc.sbuf_tensor("acst", [128, 2, NH], F32))
    asumc = es.enter_context(nc.sbuf_tensor("asumc", [NH, 1], F32))
    expasc = es.enter_context(nc.sbuf_tensor("expasc", [NH, 1], F32))
    idx32 = es.enter_context(nc.sbuf_tensor("idx32", [NH, NH], F32))
    expbc = es.enter_context(nc.sbuf_tensor("expbc", [128, NH], F32))
    dsc = es.enter_context(nc.sbuf_tensor("dsc", [NH, CH], F32))
    cbf = es.enter_context(nc.sbuf_tensor("cbf", [128, CH], BF16))
    bbf = es.enter_context(nc.sbuf_tensor("bbf", [128, CH], BF16))
    btb = es.enter_context(nc.sbuf_tensor("btb", [128, 2, 128], BF16))
    cbtm = es.enter_context(nc.sbuf_tensor("cbtm", [128, 2, CH], F32))
    ebuf = es.enter_context(nc.sbuf_tensor("ebuf", [128, 2, CH], F32))
    gbf = es.enter_context(nc.sbuf_tensor("gbf", [128, 2, CH], BF16))
    odb = es.enter_context(nc.sbuf_tensor("odb", [128, CH], F32))
    codbf = es.enter_context(nc.sbuf_tensor("codbf", [128, CH], BF16))
    sall = es.enter_context(nc.sbuf_tensor("sall", [128, DIN], F32))
    sbf = es.enter_context(nc.sbuf_tensor("sbf", [128, DIN], BF16))
    yc = es.enter_context(nc.sbuf_tensor("yc", [128, 16, CH], F32))
    sq = es.enter_context(nc.sbuf_tensor("sq", [128, CH], F32))
    ztile = es.enter_context(nc.sbuf_tensor("ztile", [128, CH], F32))
    tmp = es.enter_context(nc.sbuf_tensor("tmp", [128, CH], F32))
    ynbf = es.enter_context(nc.sbuf_tensor("ynbf", [128, 16, CH], BF16))
    woutm = es.enter_context(nc.sbuf_tensor("woutm", [128, 16, 128], BF16))
    ures = es.enter_context(nc.sbuf_tensor("ures", [128, 2, 1024], F32))
    uresT = es.enter_context(nc.sbuf_tensor("uresT", [128, 8, CH], F32))
    rsb = es.enter_context(nc.sbuf_tensor("rsb", [128, 8, CH], F32))
    mrow = es.enter_context(nc.sbuf_tensor("mrow", [1, CH], F32))
    vrow = es.enter_context(nc.sbuf_tensor("vrow", [1, CH], F32))
    m2row = es.enter_context(nc.sbuf_tensor("m2row", [1, CH], F32))
    brow = es.enter_context(nc.sbuf_tensor("brow", [1, CH], F32))
    pbig = es.enter_context(nc.psum_tensor("pbig", [128, DIN], F32))
    prow = es.enter_context(nc.psum_tensor("prow", [128, CH], F32))
    py = es.enter_context(nc.psum_tensor("py", [64, CH], F32))
    pmisc = es.enter_context(nc.psum_tensor("pmisc", [128, 512], F32))
    pab = es.enter_context(nc.psum_tensor("pab", [128, 512], F32))
    g = es.enter_context(nc.semaphore("g"))
    block = es.enter_context(nc.Block())
    p = P(nc, g)
    p.dma(idf[:, :], identf_h[:, :])
    p.dma(tri[:, :, :], tri_h.rearrange("(st p) l -> p st l", p=128))
    p.dma(sel[:, :], sel_h[:, :])
    p.dma(e2[:, :], e2_h[:, :])
    p.memset(onesf[:, :], 1.0)
    p.memset(cbias[:, 0:1], 1e-5)
    p.memset(cbias[:, 1:2], 1.0)
    p.dma(convw[:, :, :], conv_w.rearrange("(a p) j -> p a j", p=128))
    p.dma(convb[:, :], conv_b.rearrange("(a p) -> p a", p=128), slow=True)
    p.dma(nw[:, :], norm_w.rearrange("(a p) -> p a", p=128), slow=True)
    p.dma(dtb[:, :], dt_bias[:, None])
    p.dma(aneg[:, :], a_log[:, None])
    p.act(aneg[:, :], aneg[:, :], AF.Exp)
    p.ts(aneg[:, :], aneg[:, :], -1.0, ALU.mult)
    p.dma(dtall[:, :], dtraw[:, :])
    p.memset(sall[:, :], 0.0)
    for q in range(4):
        c0, c1 = q * 1024, (q + 1) * 1024
        p.ts(dtall[:, c0:c1], dtall[:, c0:c1], dtb[:, 0:1], ALU.add)
        p.ts(spa[:, :], dtall[:, c0:c1], 0.0, ALU.max)
        p.act(spb[:, :], dtall[:, c0:c1], AF.Abs)
        p.act(spb[:, :], spb[:, :], AF.Exp, scale=-1.0)
        p.act(spb[:, :], spb[:, :], AF.Ln, bias=cbias[0:NH, 1:2])
        p.tt(dtall[:, c0:c1], spa[:, :], spb[:, :], ALU.add)
    xbr = xbcp.rearrange("(a p) t -> p a t", p=128)
    zr = zf.rearrange("(a p) t -> p a t", p=128)
    urr = u.rearrange("(a p) d -> p a d", p=128)
    wor = woutb_h.rearrange("(k p) o -> p k o", p=128)
    x1r = x1.rearrange("(a p) t -> p a t", p=128)

    for c in range(NCH):
        C0 = c * CH
        if c == 0:
            p.memset(xpre[:, :, 0:3], 0.0)
            p.dma(xpre[:, :, 3:259], xbr[:, :, 0:CH])
        else:
            p.dma(xpre[:, :, 0:259], xbr[:, :, C0 - 3:C0 + CH])
        for a in range(18):
            p.ts(xconv[:, a, :], xpre[:, a, 0:CH], convw[:, a, 0:1], ALU.mult)
            for j in range(1, 4):
                p.stt(xconv[:, a, :], xpre[:, a, j:j + CH], convw[:, a, j:j + 1],
                      xconv[:, a, :], ALU.mult, ALU.add)
            p.act(xconv[:, a, :], xconv[:, a, :], AF.Silu, bias=convb[:, a:a + 1])
        p.ts(adtc[:, :], dtall[:, C0:C0 + CH], aneg[:, 0:1], ALU.mult)
        for st in range(2):
            p.trans(pmisc[:, st * 32:(st + 1) * 32], adtc[:, st * 128:(st + 1) * 128],
                    idf[0:32, 0:32])
            p.cp(adtT[:, st, :], pmisc[:, st * 32:(st + 1) * 32])
        for st in range(2):
            p.mm(prow[0:NH, :], adtT[:, st, :], tri[:, st, :],
                 start=(st == 0), stop=(st == 1), wait=(st == 0), inc=(st == 1))
        p.cp(acs[:, :], prow[0:NH, :])
        p.cp(asumc[:, :], acs[:, CH - 1:CH])
        p.act(expasc[:, :], asumc[:, :], AF.Exp)
        p.ts(idx32[:, :], idf[0:32, 0:32], expasc[:, 0:1], ALU.mult)
        p.mm(pmisc[:, 0:32], onesf[0:32, :], idx32[:, :])
        p.cp(expbc[:, :], pmisc[:, 0:32])
        p.act(dsc[:, :], acs[:, :], AF.Exp, bias=asumc[:, 0:1], scale=-1.0)
        for st in range(2):
            p.trans(pmisc[:, 64 + st * 32: 64 + (st + 1) * 32],
                    acs[:, st * 128:(st + 1) * 128], idf[0:32, 0:32])
            p.cp(acst[:, st, :], pmisc[:, 64 + st * 32: 64 + (st + 1) * 32])
        for st in range(2):
            for n4 in range(4):
                p.mm(pbig[:, n4 * 512:(n4 + 1) * 512],
                     dtall[:, C0 + st * 128: C0 + (st + 1) * 128],
                     e2[:, n4 * 512:(n4 + 1) * 512])
            for k in range(16):
                p.trans(pmisc[:, 0:128], xconv[:, k, st * 128:(st + 1) * 128], idf[:, :])
                p.cp(xdt[:, st, k * 128:(k + 1) * 128], pmisc[:, 0:128])
            p.tt(xdt[:, st, :], xdt[:, st, :], pbig[:, :], ALU.mult)
            for n4 in range(4):
                p.mm(pbig[:, n4 * 512:(n4 + 1) * 512],
                     dsc[:, st * 128:(st + 1) * 128],
                     e2[:, n4 * 512:(n4 + 1) * 512])
            p.tt(xdts[:, st, :], xdt[:, st, :], pbig[:, :], ALU.mult)
        p.cp(cbf[:, :], xconv[:, 17, :])
        p.cp(bbf[:, :], xconv[:, 16, :])
        for st in range(2):
            p.trans(pmisc[:, 0:128], xconv[:, 16, st * 128:(st + 1) * 128], idf[:, :])
            p.cp(btb[:, st, :], pmisc[:, 0:128])
        for st in range(2):
            p.mm(pmisc[:, 0:CH], bbf[:, st * 128:(st + 1) * 128], cbf[:, :])
            p.tt(cbtm[:, st, :], pmisc[:, 0:CH], tri[:, st, :], ALU.mult)
        p.cp(sbf[:, :], sall[:, :])
        for h in range(NH):
            hp0 = h * HD
            p.mm(prow[:, :], sel[:, h * 128:(h + 1) * 128], acs[:, :])
            p.act(odb[:, :], prow[:, :], AF.Exp)
            p.tt(codbf[:, :], odb[:, :], xconv[:, 17, :], ALU.mult)
            for st in range(2):
                p.ts(ebuf[:, st, :], prow[:, :], acst[:, st, h:h + 1], ALU.subtract,
                     0.0, ALU.min)
                p.act(ebuf[:, st, :], ebuf[:, st, :], AF.Exp)
                p.tt(gbf[:, st, :], ebuf[:, st, :], cbtm[:, st, :], ALU.mult)
            p.mm(py[:, :], xdt[:, 0, hp0:hp0 + HD], gbf[:, 0, :],
                 start=True, stop=False, wait=True, inc=False)
            p.mm(py[:, :], xdt[:, 1, hp0:hp0 + HD], gbf[:, 1, :],
                 start=False, stop=False, wait=False, inc=False)
            p.mm(py[:, :], sbf[:, hp0:hp0 + HD], codbf[:, :],
                 start=False, stop=True, wait=False, inc=True)
            p.mm(pbig[:, hp0:hp0 + HD], btb[:, 0, :], xdts[:, 0, hp0:hp0 + HD],
                 start=True, stop=False, wait=True, inc=False)
            p.mm(pbig[:, hp0:hp0 + HD], btb[:, 1, :], xdts[:, 1, hp0:hp0 + HD],
                 start=False, stop=True, wait=False, inc=True)
            r0 = (h % 2) * 64
            p.cp(yc[r0:r0 + 64, h // 2, :], py[:, :])
        for h in range(NH):
            hp0 = h * HD
            p.stt(sall[:, hp0:hp0 + HD], sall[:, hp0:hp0 + HD], expbc[:, h:h + 1],
                  pbig[:, hp0:hp0 + HD], ALU.mult, ALU.add)
        for a in range(16):
            p.mm(pab[0:1, 0:CH], onesf[:, 0:1], yc[:, a, :],
                 start=(a == 0), stop=(a == 15), wait=(a == 0), inc=(a == 15))
        p.ts(mrow[:, :], pab[0:1, 0:CH], 1.0 / DIN, ALU.mult)
        for a in range(16):
            p.act(sq[:, :], yc[:, a, :], AF.Square)
            p.mm(pab[0:1, CH:2 * CH], onesf[:, 0:1], sq[:, :],
                 start=(a == 0), stop=(a == 15))
        p.ts(vrow[:, :], pab[0:1, CH:2 * CH], 1.0 / DIN, ALU.mult)
        p.act(m2row[:, :], mrow[:, :], AF.Square)
        p.tt(vrow[:, :], vrow[:, :], m2row[:, :], ALU.subtract)
        p.act(vrow[:, :], vrow[:, :], AF.Ln, bias=cbias[0:1, 0:1])
        p.act(vrow[:, :], vrow[:, :], AF.Exp, scale=-0.5)
        p.tt(brow[:, :], mrow[:, :], vrow[:, :], ALU.mult)
        p.ts(brow[:, :], brow[:, :], -1.0, ALU.mult)
        p.mm(pab[:, 0:CH], onesf[0:1, :], vrow[:, :])
        p.mm(pab[:, CH:2 * CH], onesf[0:1, :], brow[:, :])
        for a in range(16):
            p.dma(ztile[:, :], zr[:, a, C0:C0 + CH])
            p.tt(tmp[:, :], yc[:, a, :], pab[:, 0:CH], ALU.mult)
            p.tt(tmp[:, :], tmp[:, :], pab[:, CH:2 * CH], ALU.add)
            p.ts(tmp[:, :], tmp[:, :], nw[:, a:a + 1], ALU.mult)
            p.tt(ynbf[:, a, :], tmp[:, :], ztile[:, :], ALU.mult)
        p.dma(ures[:, :, :], urr[:, 2 * c:2 * c + 2, :])
        for j in range(2):
            for k in range(8):
                p.trans(pmisc[:, 0:128], ures[:, j, k * 128:(k + 1) * 128], idf[:, :])
                p.cp(uresT[:, k, j * 128:(j + 1) * 128], pmisc[:, 0:128])
        for m in range(8):
            p.dma(woutm[:, :, :], wor[:, :, m * 128:(m + 1) * 128])
            for k in range(16):
                p.mm(pmisc[:, CH:2 * CH], woutm[:, k, :], ynbf[:, k, :],
                     start=(k == 0), stop=(k == 15), wait=(k == 0), inc=(k == 15))
            p.tt(rsb[:, m, :], pmisc[:, CH:2 * CH], uresT[:, m, :], ALU.add)
        for m in range(8):
            p.mm(pab[0:1, 0:CH], onesf[:, 0:1], rsb[:, m, :],
                 start=(m == 0), stop=(m == 7), wait=(m == 0), inc=(m == 7))
        p.ts(mrow[:, :], pab[0:1, 0:CH], 1.0 / D, ALU.mult)
        for m in range(8):
            p.act(sq[:, :], rsb[:, m, :], AF.Square)
            p.mm(pab[0:1, CH:2 * CH], onesf[:, 0:1], sq[:, :],
                 start=(m == 0), stop=(m == 7))
        p.ts(vrow[:, :], pab[0:1, CH:2 * CH], 1.0 / D, ALU.mult)
        p.act(m2row[:, :], mrow[:, :], AF.Square)
        p.tt(vrow[:, :], vrow[:, :], m2row[:, :], ALU.subtract)
        p.act(vrow[:, :], vrow[:, :], AF.Ln, bias=cbias[0:1, 0:1])
        p.act(vrow[:, :], vrow[:, :], AF.Exp, scale=-0.5)
        p.tt(brow[:, :], mrow[:, :], vrow[:, :], ALU.mult)
        p.ts(brow[:, :], brow[:, :], -1.0, ALU.mult)
        p.mm(pab[:, 0:CH], onesf[0:1, :], vrow[:, :])
        p.mm(pab[:, CH:2 * CH], onesf[0:1, :], brow[:, :])
        for m in range(8):
            p.tt(rsb[:, m, :], rsb[:, m, :], pab[:, 0:CH], ALU.mult)
            p.tt(rsb[:, m, :], rsb[:, m, :], pab[:, CH:2 * CH], ALU.add)
        p.dma(x1r[:, :, C0:C0 + CH], rsb[:, :, :], eng="gpsimd")
    p.finalize(block)
    es.close()
    return nc


def build_B():
    nc = new_prog("B")
    x1h = nc.dram_tensor("x1", [D, L], F32, kind="ExternalInput").ap()
    w1t = nc.dram_tensor("w1t", [D, 4096], F32, kind="ExternalInput").ap()
    b1h = nc.dram_tensor("b1", [4096], F32, kind="ExternalInput").ap()
    w2t = nc.dram_tensor("w2t", [4096, D], F32, kind="ExternalInput").ap()
    b2h = nc.dram_tensor("b2", [D], F32, kind="ExternalInput").ap()
    sc4 = nc.dram_tensor("sc4", [4, 1], F32, kind="ExternalInput").ap()
    identf_h = nc.dram_tensor("identf", [128, 128], F32, kind="ExternalInput").ap()
    x2 = nc.dram_tensor("x2", [D, L], F32, kind="ExternalOutput").ap()
    w2b = nc.dram_tensor("w2b", [4096, D], BF16, kind="Internal").ap()

    es = ExitStack()
    w1b = es.enter_context(nc.sbuf_tensor("w1b", [128, 8, 4096], BF16))
    stg = es.enter_context(nc.sbuf_tensor("stg", [128, 2048], F32))
    w2m = es.enter_context(nc.sbuf_tensor("w2m", [128, 32, 128], BF16))
    x1n = es.enter_context(nc.sbuf_tensor("x1n", [128, 8, 512], F32))
    x1nb = es.enter_context(nc.sbuf_tensor("x1nb", [128, 8, 512], BF16))
    hact = es.enter_context(nc.sbuf_tensor("hact", [128, 32, 512], BF16))
    ltT = es.enter_context(nc.sbuf_tensor("ltT", [16, 1024], F32))
    ltTb = es.enter_context(nc.sbuf_tensor("ltTb", [16, 1024], BF16))
    e1b = es.enter_context(nc.sbuf_tensor("e1b", [4, 4096], BF16))
    fb = es.enter_context(nc.sbuf_tensor("fb", [16, 1024], BF16))
    o03 = es.enter_context(nc.sbuf_tensor("o03", [128, 8, 4], BF16))
    o47 = es.enter_context(nc.sbuf_tensor("o47", [128, 8, 4], BF16))
    t1s = es.enter_context(nc.sbuf_tensor("t1s", [4, 512], BF16))
    s2t = es.enter_context(nc.sbuf_tensor("s2t", [4, 4, 512], F32))
    s2tb = es.enter_context(nc.sbuf_tensor("s2tb", [4, 4, 512], BF16))
    s2all = es.enter_context(nc.sbuf_tensor("s2all", [16, 512], BF16))
    b1c = es.enter_context(nc.sbuf_tensor("b1c", [128, 32], F32))
    b2c = es.enter_context(nc.sbuf_tensor("b2c", [128, 8], F32))
    sc4s = es.enter_context(nc.sbuf_tensor("sc4s", [4, 1], F32))
    rsb = es.enter_context(nc.sbuf_tensor("rsb", [128, 8, 512], F32))
    sq = es.enter_context(nc.sbuf_tensor("sq", [128, 512], F32))
    onesf = es.enter_context(nc.sbuf_tensor("onesf", [128, 128], F32))
    cbias = es.enter_context(nc.sbuf_tensor("cbias", [128, 2], F32))
    idf = es.enter_context(nc.sbuf_tensor("idf", [128, 128], F32))
    mrow = es.enter_context(nc.sbuf_tensor("mrow", [1, 512], F32))
    vrow = es.enter_context(nc.sbuf_tensor("vrow", [1, 512], F32))
    m2row = es.enter_context(nc.sbuf_tensor("m2row", [1, 512], F32))
    brow = es.enter_context(nc.sbuf_tensor("brow", [1, 512], F32))
    pg = es.enter_context(nc.psum_tensor("pg", [128, 512], F32))
    psm = es.enter_context(nc.psum_tensor("psm", [16, 512], F32))
    pab = es.enter_context(nc.psum_tensor("pab", [128, 1024], F32))
    g = es.enter_context(nc.semaphore("g"))
    block = es.enter_context(nc.Block())
    p = P(nc, g)
    p.memset(onesf[:, :], 1.0)
    p.memset(cbias[:, 0:1], 1e-5)
    p.memset(cbias[:, 1:2], 1.0)
    p.dma(idf[:, :], identf_h[:, :])
    p.dma(b1c[:, :], b1h.rearrange("(a p) -> p a", p=128), slow=True)
    p.dma(b2c[:, :], b2h.rearrange("(a p) -> p a", p=128), slow=True)
    p.dma(sc4s[:, :], sc4[:, :])
    w2r = w2t.rearrange("(a p) o -> p a o", p=128)
    w2br = w2b.rearrange("(a p) o -> p a o", p=128)
    for a in range(32):
        p.dma(stg[:, 0:1024], w2r[:, a, :])
        p.cp(hact[:, 0, 0:512], stg[:, 0:512])
        p.cp(hact[:, 1, 0:512], stg[:, 512:1024])
        p.dma(w2br[:, a, 0:512], hact[:, 0, 0:512], eng="gpsimd")
        p.dma(w2br[:, a, 512:1024], hact[:, 1, 0:512], eng="gpsimd")
    w1r = w1t.rearrange("(k p) f -> p k f", p=128)
    for k in range(8):
        for hhalf in range(2):
            p.dma(stg[:, :], w1r[:, k, hhalf * 2048:(hhalf + 1) * 2048])
            p.cp(w1b[:, k, hhalf * 2048:(hhalf + 1) * 2048], stg[:, :])
    x1rr = x1h.rearrange("(a p) t -> p a t", p=128)
    x2rr = x2.rearrange("(a p) t -> p a t", p=128)
    for nt in range(8):
        T0 = nt * 512
        p.dma(x1n[:, :, :], x1rr[:, :, T0:T0 + 512])
        p.cp(x1nb[:, :, :], x1n[:, :, :])
        if nt == 0:
            for a in range(8):
                p.trans(psm[:, 0:128], x1n[:, a, 1:17], idf[:, :])
                p.cp(ltT[:, a * 128:(a + 1) * 128], psm[:, 0:128])
            p.cp(ltTb[:, :], ltT[:, :])
            p.memset(e1b[:, :], 0.0)
            p.memset(fb[:, :], 0.0)
            for i in range(4):
                p.dma(e1b[i:i + 1, i * 1024:(i + 1) * 1024], ltTb[2 * i:2 * i + 1, :],
                      eng="gpsimd")
            for mq in range(4):
                for i in range(4):
                    p.dma(fb[mq * 4 + i:mq * 4 + i + 1, i * 256:(i + 1) * 256],
                          ltTb[8 + 2 * i:9 + 2 * i, mq:1024:4], eng="gpsimd", slow=True)
            for a in range(8):
                p.cp(o03[:, a, :], x1n[:, a, 2:9:2])
                p.cp(o47[:, a, :], x1n[:, a, 10:17:2])
        for k in range(8):
            p.mm(psm[0:4, :], o03[:, k, :], x1nb[:, k, :],
                 start=(k == 0), stop=(k == 7), wait=(k == 0), inc=(k == 7))
        p.ts(t1s[:, :], psm[0:4, :], sc4s[0:4, 0:1], ALU.mult)
        for m in range(32):
            for k in range(8):
                p.mm(pg[:, :], w1b[:, k, m * 128:(m + 1) * 128],
                     x1nb[:, k, :], start=(k == 0), stop=False,
                     wait=(k == 0), inc=False)
            p.mm(pg[:, :], e1b[0:4, m * 128:(m + 1) * 128], t1s[:, :],
                 start=False, stop=True, wait=False, inc=True)
            p.act(hact[:, m, :], pg[:, :], AF.Gelu, bias=b1c[:, m:m + 1])
        for mq in range(4):
            for k in range(8):
                p.mm(psm[0:4, :], o47[:, k, :], hact[:, mq * 8 + k, :],
                     start=(k == 0), stop=(k == 7), wait=(k == 0), inc=(k == 7))
            p.ts(s2t[:, mq, :], psm[0:4, :], sc4s[0:4, 0:1], ALU.mult)
        p.cp(s2tb[:, :, :], s2t[:, :, :])
        for mq in range(4):
            p.dma(s2all[mq * 4:(mq + 1) * 4, :], s2tb[:, mq, :], eng="gpsimd")
        for m in range(8):
            p.dma(w2m[:, :, :], w2br[:, :, m * 128:(m + 1) * 128])
            for k in range(32):
                p.mm(pg[:, :], w2m[:, k, :], hact[:, k, :],
                     start=(k == 0), stop=False, wait=(k == 0), inc=False)
            p.mm(pg[:, :], fb[:, m * 128:(m + 1) * 128], s2all[:, :],
                 start=False, stop=True, wait=False, inc=True)
            p.stt(rsb[:, m, :], pg[:, :], b2c[:, m:m + 1], x1n[:, m, :],
                  ALU.add, ALU.add)
        for m in range(8):
            p.mm(pab[0:1, 0:512], onesf[:, 0:1], rsb[:, m, :],
                 start=(m == 0), stop=(m == 7), wait=(m == 0), inc=(m == 7))
        p.ts(mrow[:, :], pab[0:1, 0:512], 1.0 / D, ALU.mult)
        for m in range(8):
            p.act(sq[:, :], rsb[:, m, :], AF.Square)
            p.mm(pab[0:1, 512:1024], onesf[:, 0:1], sq[:, :],
                 start=(m == 0), stop=(m == 7))
        p.ts(vrow[:, :], pab[0:1, 512:1024], 1.0 / D, ALU.mult)
        p.act(m2row[:, :], mrow[:, :], AF.Square)
        p.tt(vrow[:, :], vrow[:, :], m2row[:, :], ALU.subtract)
        p.act(vrow[:, :], vrow[:, :], AF.Ln, bias=cbias[0:1, 0:1])
        p.act(vrow[:, :], vrow[:, :], AF.Exp, scale=-0.5)
        p.tt(brow[:, :], mrow[:, :], vrow[:, :], ALU.mult)
        p.ts(brow[:, :], brow[:, :], -1.0, ALU.mult)
        p.mm(pab[:, 0:512], onesf[0:1, :], vrow[:, :])
        p.mm(pab[:, 512:1024], onesf[0:1, :], brow[:, :])
        for m in range(8):
            p.tt(rsb[:, m, :], rsb[:, m, :], pab[:, 0:512], ALU.mult)
            p.tt(rsb[:, m, :], rsb[:, m, :], pab[:, 512:1024], ALU.add)
        p.dma(x2rr[:, :, T0:T0 + 512], rsb[:, :, :], eng="gpsimd")
    p.finalize(block)
    es.close()
    return nc


def build_C():
    nc = new_prog("C")
    x2h = nc.dram_tensor("x2", [D, L], F32, kind="ExternalInput").ap()
    wqk = nc.dram_tensor("wqk", [D, 2048], F32, kind="ExternalInput").ap()
    bqk = nc.dram_tensor("bqk", [2048], F32, kind="ExternalInput").ap()
    wv = nc.dram_tensor("wv", [D, D], F32, kind="ExternalInput").ap()
    bv = nc.dram_tensor("bv", [D], F32, kind="ExternalInput").ap()
    x3 = nc.dram_tensor("x3", [D, L], F32, kind="ExternalOutput").ap()
    vth = nc.dram_tensor("vth", [L, D], BF16, kind="Internal").ap()
    attn = nc.dram_tensor("attn", [D, L], F32, kind="Internal").ap()

    es = ExitStack()
    x2b = es.enter_context(nc.sbuf_tensor("x2b", [128, 8, L], BF16))
    stg = es.enter_context(nc.sbuf_tensor("stg", [128, 4096], F32))
    esb = es.enter_context(nc.sbuf_tensor("esb", [128, 32, 512], BF16))
    vhh = es.enter_context(nc.sbuf_tensor("vhh", [128, 32, 64], BF16))
    qs = es.enter_context(nc.sbuf_tensor("qs", [64, L], BF16))
    ks = es.enter_context(nc.sbuf_tensor("ks", [64, L], BF16))
    sqb = es.enter_context(nc.sbuf_tensor("sqb", [64, L], BF16))
    wq = es.enter_context(nc.sbuf_tensor("wq", [128, 8, 64], BF16))
    wk = es.enter_context(nc.sbuf_tensor("wk", [128, 8, 64], BF16))
    wvb = es.enter_context(nc.sbuf_tensor("wvb", [128, 8, 1024], BF16))
    vstage = es.enter_context(nc.sbuf_tensor("vstage", [128, 1024], BF16))
    bqkb = es.enter_context(nc.sbuf_tensor("bqkb", [1, 2048], BF16))
    bvb = es.enter_context(nc.sbuf_tensor("bvb", [1, 1024], BF16))
    onesb = es.enter_context(nc.sbuf_tensor("onesb", [128, 512], BF16))
    onesf = es.enter_context(nc.sbuf_tensor("onesf", [128, 128], F32))
    cbias = es.enter_context(nc.sbuf_tensor("cbias", [128, 2], F32))
    nq8 = es.enter_context(nc.sbuf_tensor("nq8", [1, 8], F32))
    nqm = es.enter_context(nc.sbuf_tensor("nqm", [1, 1], F32))
    nkm = es.enter_context(nc.sbuf_tensor("nkm", [1, 1], F32))
    nmb = es.enter_context(nc.sbuf_tensor("nmb", [1, 1], BF16))
    nmc = es.enter_context(nc.sbuf_tensor("nmc", [128, 1], F32))
    rz = es.enter_context(nc.sbuf_tensor("rz", [1, 512], F32))
    rzb = es.enter_context(nc.sbuf_tensor("rzb", [1, 512], BF16))
    osb = es.enter_context(nc.sbuf_tensor("osb", [64, 512], F32))
    oo = es.enter_context(nc.sbuf_tensor("oo", [64, 512], F32))
    at = es.enter_context(nc.sbuf_tensor("at", [128, 8, 512], F32))
    sq = es.enter_context(nc.sbuf_tensor("sq", [128, 512], F32))
    mrow = es.enter_context(nc.sbuf_tensor("mrow", [1, 512], F32))
    vrow = es.enter_context(nc.sbuf_tensor("vrow", [1, 512], F32))
    m2row = es.enter_context(nc.sbuf_tensor("m2row", [1, 512], F32))
    brow = es.enter_context(nc.sbuf_tensor("brow", [1, 512], F32))
    pv = es.enter_context(nc.psum_tensor("pv", [128, 1024], F32))
    pa = es.enter_context(nc.psum_tensor("pa", [128, 512], F32))
    pq = es.enter_context(nc.psum_tensor("pq", [64, 512], F32))
    pzz = es.enter_context(nc.psum_tensor("pzz", [1, 512], F32))
    po = es.enter_context(nc.psum_tensor("po", [64, 512], F32))
    pn = es.enter_context(nc.psum_tensor("pn", [128, 512], F32))
    g = es.enter_context(nc.semaphore("g"))
    block = es.enter_context(nc.Block())
    p = P(nc, g)
    p.memset(onesb[:, :], 1.0)
    p.memset(onesf[:, :], 1.0)
    p.memset(cbias[:, 0:1], 1e-5)
    p.memset(cbias[:, 1:2], 1.0)
    x2rr = x2h.rearrange("(a p) t -> p a t", p=128)
    for k in range(8):
        p.dma(stg[:, :], x2rr[:, k, :])
        p.cp(x2b[:, k, :], stg[:, :])
    wvr = wv.rearrange("(k p) o -> p k o", p=128)
    for k in range(8):
        p.dma(stg[:, 0:1024], wvr[:, k, :])
        p.cp(wvb[:, k, :], stg[:, 0:1024])
    p.dma(stg[0:1, 0:2048], bqk[None, :])
    p.cp(bqkb[:, :], stg[0:1, 0:2048])
    p.dma(stg[0:1, 2048:3072], bv[None, :])
    p.cp(bvb[:, :], stg[0:1, 2048:3072])
    for st in range(32):
        for k in range(8):
            for nh_ in range(2):
                p.mm(pv[:, nh_ * 512:(nh_ + 1) * 512],
                     x2b[:, k, st * 128:(st + 1) * 128],
                     wvb[:, k, nh_ * 512:(nh_ + 1) * 512],
                     start=(k == 0), stop=False,
                     wait=(k == 0 and nh_ == 0), inc=False)
        for nh_ in range(2):
            p.mm(pv[:, nh_ * 512:(nh_ + 1) * 512], onesb[0:1, 0:128],
                 bvb[0:1, nh_ * 512:(nh_ + 1) * 512],
                 start=False, stop=True, wait=False, inc=(nh_ == 1))
        p.ts(vstage[:, :], pv[:, :], 1.0 / 32.0, ALU.mult)
        p.dma(vth[st * 128:(st + 1) * 128, :], vstage[:, :], eng="gpsimd")
    wqr = wqk.rearrange("(k p) f -> p k f", p=128)
    vtr = vth.rearrange("(st p) c -> p st c", p=128)
    for h in range(16):
        p.dma(stg[:, 0:512], wqr[:, :, h * 128:h * 128 + 64])
        p.cp(wq[:, :, :], stg[:, 0:512].rearrange("p (k c) -> p k c", k=8))
        p.dma(stg[:, 0:512], wqr[:, :, h * 128 + 64:(h + 1) * 128])
        p.cp(wk[:, :, :], stg[:, 0:512].rearrange("p (k c) -> p k c", k=8))
        for nt in range(8):
            for k in range(8):
                p.mm(pq[:, :], wq[:, k, :], x2b[:, k, nt * 512:(nt + 1) * 512],
                     start=(k == 0), stop=False, wait=(k == 0), inc=False)
            p.mm(pq[:, :], bqkb[0:1, h * 128:h * 128 + 64],
                 onesb[0:1, 0:512], start=False, stop=True, wait=False, inc=True)
            p.ts(qs[:, nt * 512:(nt + 1) * 512], pq[:, :], 1.0 / 65536.0, ALU.mult)
        for nt in range(8):
            for k in range(8):
                p.mm(pq[:, :], wk[:, k, :], x2b[:, k, nt * 512:(nt + 1) * 512],
                     start=(k == 0), stop=False, wait=(k == 0), inc=False)
            p.mm(pq[:, :], bqkb[0:1, h * 128 + 64:(h + 1) * 128],
                 onesb[0:1, 0:512], start=False, stop=True, wait=False, inc=True)
            p.cp(ks[:, nt * 512:(nt + 1) * 512], pq[:, :])
        for which, outm in ((qs, nqm), (ks, nkm)):
            p.act(sqb[:, :], which[:, :], AF.Square)
            for nt in range(8):
                p.mm(pzz[:, :], onesb[0:64, 0:1], sqb[:, nt * 512:(nt + 1) * 512])
                p.red(nq8[0:1, nt:nt + 1], pzz[0:1, :], ALU.max)
            p.red(outm[:, :], nq8[:, :], ALU.max)
            p.act(outm[:, :], outm[:, :], AF.Sqrt)
        p.tt(nqm[:, :], nqm[:, :], nkm[:, :], ALU.mult)
        p.ts(nqm[:, :], nqm[:, :], -1.0, ALU.mult)
        p.cp(nmb[:, :], nqm[:, :])
        p.mm(pn[:, 0:1], onesb[0:1, 0:128], nmb[0:1, :])
        p.cp(nmc[:, :], pn[:, 0:1])
        p.dma(vhh[:, :, :], vtr[:, :, h * 64:(h + 1) * 64])
        for lb in range(8):
            L0 = lb * 512
            for st in range(32):
                p.mm(pa[:, :], ks[:, st * 128:(st + 1) * 128], qs[:, L0:L0 + 512])
                p.act(esb[:, st, :], pa[:, :], AF.Exp, bias=nmc[:, 0:1])
                p.mm(pzz[:, :], onesb[:, 0:1], esb[:, st, :],
                     start=(st == 0), stop=(st == 31))
                p.mm(po[:, :], vhh[:, st, :], esb[:, st, :],
                     start=(st == 0), stop=(st == 31))
            p.recip(rz[:, :], pzz[0:1, :])
            p.cp(rzb[:, :], rz[:, :])
            p.mm(pn[0:64, :], onesb[0:1, 0:64], rzb[0:1, :])
            p.cp(osb[:, :], po[:, :])
            p.tt(oo[:, :], osb[:, :], pn[0:64, :], ALU.mult)
            p.dma(attn[h * 64:(h + 1) * 64, L0:L0 + 512], oo[:, :], eng="gpsimd")
    arr = attn.rearrange("(a p) t -> p a t", p=128)
    x3rr = x3.rearrange("(a p) t -> p a t", p=128)
    for nt in range(8):
        T0 = nt * 512
        p.dma(at[:, :, :], arr[:, :, T0:T0 + 512])
        stgv = stg.rearrange("p (a t) -> p a t", a=8)
        p.dma(stgv[:, :, :], x2rr[:, :, T0:T0 + 512])
        for m in range(8):
            p.tt(at[:, m, :], at[:, m, :], stgv[:, m, :], ALU.add)
        for m in range(8):
            p.mm(pv[0:1, 0:512], onesf[:, 0:1], at[:, m, :],
                 start=(m == 0), stop=(m == 7), wait=(m == 0), inc=(m == 7))
        p.ts(mrow[:, :], pv[0:1, 0:512], 1.0 / D, ALU.mult)
        for m in range(8):
            p.act(sq[:, :], at[:, m, :], AF.Square)
            p.mm(pv[0:1, 512:1024], onesf[:, 0:1], sq[:, :],
                 start=(m == 0), stop=(m == 7))
        p.ts(vrow[:, :], pv[0:1, 512:1024], 1.0 / D, ALU.mult)
        p.act(m2row[:, :], mrow[:, :], AF.Square)
        p.tt(vrow[:, :], vrow[:, :], m2row[:, :], ALU.subtract)
        p.act(vrow[:, :], vrow[:, :], AF.Ln, bias=cbias[0:1, 0:1])
        p.act(vrow[:, :], vrow[:, :], AF.Exp, scale=-0.5)
        p.tt(brow[:, :], mrow[:, :], vrow[:, :], ALU.mult)
        p.ts(brow[:, :], brow[:, :], -1.0, ALU.mult)
        p.mm(pv[:, 0:512], onesf[0:1, :], vrow[:, :])
        p.mm(pv[:, 512:1024], onesf[0:1, :], brow[:, :])
        for m in range(8):
            p.tt(at[:, m, :], at[:, m, :], pv[:, 0:512], ALU.mult)
            p.tt(at[:, m, :], at[:, m, :], pv[:, 512:1024], ALU.add)
        p.dma(x3rr[:, :, T0:T0 + 512], at[:, :, :], eng="gpsimd")
    p.finalize(block)
    es.close()
    return nc


def build_D(p_idx):
    nc = new_prog("D%d" % p_idx)
    T0 = p_idx * 2048
    x3h = nc.dram_tensor("x3", [D, L], F32, kind="ExternalInput").ap()
    ws1 = nc.dram_tensor("ws1", [D, D], F32, kind="ExternalInput").ap()
    bs1 = nc.dram_tensor("bs1", [D], F32, kind="ExternalInput").ap()
    ws2 = nc.dram_tensor("ws2", [D, D], F32, kind="ExternalInput").ap()
    bs2 = nc.dram_tensor("bs2", [D], F32, kind="ExternalInput").ap()
    we1 = nc.dram_tensor("we1", [31, D, D], F32, kind="ExternalInput").ap()
    be1 = nc.dram_tensor("be1", [31, D], F32, kind="ExternalInput").ap()
    we2 = nc.dram_tensor("we2", [31, D, D], F32, kind="ExternalInput").ap()
    be2 = nc.dram_tensor("be2", [31, D], F32, kind="ExternalInput").ap()
    wr = nc.dram_tensor("wr", [D, 31], F32, kind="ExternalInput").ap()
    br = nc.dram_tensor("br", [31], F32, kind="ExternalInput").ap()
    identf_h = nc.dram_tensor("identf", [128, 128], F32, kind="ExternalInput").ap()
    outp = nc.dram_tensor("outp", [2048, D], F32, kind="ExternalOutput").ap()

    es = ExitStack()
    x3b = es.enter_context(nc.sbuf_tensor("x3b", [128, 8, 2048], BF16))
    stg = es.enter_context(nc.sbuf_tensor("stg", [128, 4096], F32))
    w1e = es.enter_context(nc.sbuf_tensor("w1e", [128, 8, 1024], BF16))
    w2e = es.enter_context(nc.sbuf_tensor("w2e", [128, 8, 1024], BF16))
    hact = es.enter_context(nc.sbuf_tensor("hact", [128, 8, 512], BF16))
    acc = es.enter_context(nc.sbuf_tensor("acc", [128, 8, 2048], F32))
    G128 = es.enter_context(nc.sbuf_tensor("G128", [128, 2048], BF16))
    g_row = es.enter_context(nc.sbuf_tensor("g_row", [1, 2048], F32))
    g_rowb = es.enter_context(nc.sbuf_tensor("g_rowb", [1, 2048], BF16))
    gatesT = es.enter_context(nc.sbuf_tensor("gatesT", [31, 2048], F32))
    gtok = es.enter_context(nc.sbuf_tensor("gtok", [128, 31], F32))
    aff = es.enter_context(nc.sbuf_tensor("aff", [128, 31], F32))
    aff2 = es.enter_context(nc.sbuf_tensor("aff2", [128, 31], F32))
    eqm = es.enter_context(nc.sbuf_tensor("eqm", [128, 31], F32))
    mx = es.enter_context(nc.sbuf_tensor("mx", [128, 1], F32))
    wrb = es.enter_context(nc.sbuf_tensor("wrb", [128, 8, 31], BF16))
    brb = es.enter_context(nc.sbuf_tensor("brb", [1, 31], BF16))
    be1c = es.enter_context(nc.sbuf_tensor("be1c", [128, 8], F32))
    bstg = es.enter_context(nc.sbuf_tensor("bstg", [1, 1024], F32))
    be2rb = es.enter_context(nc.sbuf_tensor("be2rb", [1, 1024], BF16))
    x3s = es.enter_context(nc.sbuf_tensor("x3s", [128, 512], F32))
    outc = es.enter_context(nc.sbuf_tensor("outc", [128, 1024], F32))
    sq = es.enter_context(nc.sbuf_tensor("sq", [128, 512], F32))
    onesf = es.enter_context(nc.sbuf_tensor("onesf", [128, 128], F32))
    cbias = es.enter_context(nc.sbuf_tensor("cbias", [128, 2], F32))
    onesb = es.enter_context(nc.sbuf_tensor("onesb", [128, 128], BF16))
    idf = es.enter_context(nc.sbuf_tensor("idf", [128, 128], F32))
    mrow = es.enter_context(nc.sbuf_tensor("mrow", [1, 512], F32))
    vrow = es.enter_context(nc.sbuf_tensor("vrow", [1, 512], F32))
    m2row = es.enter_context(nc.sbuf_tensor("m2row", [1, 512], F32))
    brow = es.enter_context(nc.sbuf_tensor("brow", [1, 512], F32))
    pg = es.enter_context(nc.psum_tensor("pg", [128, 512], F32))
    pgt = es.enter_context(nc.psum_tensor("pgt", [128, 31], F32))
    ptr = es.enter_context(nc.psum_tensor("ptr", [128, 128], F32))
    pab = es.enter_context(nc.psum_tensor("pab", [128, 1024], F32))
    g = es.enter_context(nc.semaphore("g"))
    block = es.enter_context(nc.Block())
    p = P(nc, g)
    p.memset(onesf[:, :], 1.0)
    p.memset(cbias[:, 0:1], 1e-5)
    p.memset(cbias[:, 1:2], 1.0)
    p.memset(onesb[:, :], 1.0)
    p.memset(acc[:, :, :], 0.0)
    p.dma(idf[:, :], identf_h[:, :])
    x3rr = x3h.rearrange("(a p) t -> p a t", p=128)
    for k in range(8):
        p.dma(stg[:, 0:2048], x3rr[:, k, T0:T0 + 2048])
        p.cp(x3b[:, k, :], stg[:, 0:2048])
    p.dma(stg[:, 0:248], wr.rearrange("(k p) e -> p k e", p=128))
    p.cp(wrb[:, :, :], stg[:, 0:248].rearrange("p (k e) -> p k e", k=8))
    p.dma(bstg[0:1, 0:31], br[None, :])
    p.cp(brb[:, :], bstg[0:1, 0:31])
    for tt_ in range(16):
        t0 = tt_ * 128
        for k in range(8):
            p.mm(pgt[:, :], x3b[:, k, t0:t0 + 128], wrb[:, k, :],
                 start=(k == 0), stop=False, wait=(k == 0), inc=False)
        p.mm(pgt[:, :], onesb[0:1, :], brb[0:1, :],
             start=False, stop=True, wait=False, inc=True)
        p.red(mx[:, :], pgt[:, :], ALU.max)
        p.ts(mx[:, :], mx[:, :], -1.0, ALU.mult)
        p.act(aff[:, :], pgt[:, :], AF.Exp, bias=mx[:, 0:1])
        p.red(mx[:, :], aff[:, :], ALU.add)
        p.recip(mx[:, :], mx[:, :])
        p.ts(aff[:, :], aff[:, :], mx[:, 0:1], ALU.mult)
        p.red(mx[:, :], aff[:, :], ALU.max)
        p.ts(eqm[:, :], aff[:, :], mx[:, 0:1], ALU.is_equal)
        p.stt(aff2[:, :], eqm[:, :], -1e30, aff[:, :], ALU.mult, ALU.add)
        p.red(mx[:, :], aff2[:, :], ALU.max)
        p.ts(eqm[:, :], aff2[:, :], mx[:, 0:1], ALU.is_equal)
        p.stt(aff2[:, :], eqm[:, :], -1e30, aff2[:, :], ALU.mult, ALU.add)
        p.red(mx[:, :], aff2[:, :], ALU.max)
        p.ts(eqm[:, :], aff[:, :], mx[:, 0:1], ALU.is_ge)
        p.tt(gtok[:, :], aff[:, :], eqm[:, :], ALU.mult)
        p.trans(ptr[0:31, 0:128], gtok[:, :], idf[:, :])
        p.cp(gatesT[:, t0:t0 + 128], ptr[0:31, 0:128])
    we1r = we1.rearrange("e (k p) o -> p e k o", p=128)
    we2r = we2.rearrange("e (k p) o -> p e k o", p=128)
    ws1r = ws1.rearrange("(k p) o -> p k o", p=128)
    ws2r = ws2.rearrange("(k p) o -> p k o", p=128)
    be1r = be1.rearrange("e (a p) -> p e a", p=128)
    stg4 = stg.rearrange("p (k o) -> p k o", k=4)
    for e in range(32):
        if e < 31:
            for kh in range(2):
                p.dma(stg[:, :], we1r[:, e, kh * 4:(kh + 1) * 4, :])
                p.cp(w1e[:, kh * 4:(kh + 1) * 4, :], stg4[:, :, :])
            for kh in range(2):
                p.dma(stg[:, :], we2r[:, e, kh * 4:(kh + 1) * 4, :])
                p.cp(w2e[:, kh * 4:(kh + 1) * 4, :], stg4[:, :, :])
            p.dma(be1c[:, :], be1r[:, e, :], slow=True)
            p.dma(bstg[0:1, :], be2[e:e + 1, :])
            p.cp(be2rb[:, :], bstg[0:1, :])
            p.dma(g_row[:, :], gatesT[e:e + 1, :], eng="gpsimd")
            p.cp(g_rowb[:, :], g_row[:, :])
            for tb in range(4):
                p.mm(pg[:, :], onesb[0:1, :], g_rowb[0:1, tb * 512:(tb + 1) * 512])
                p.cp(G128[:, tb * 512:(tb + 1) * 512], pg[:, :])
        else:
            for kh in range(2):
                p.dma(stg[:, :], ws1r[:, kh * 4:(kh + 1) * 4, :])
                p.cp(w1e[:, kh * 4:(kh + 1) * 4, :], stg4[:, :, :])
            for kh in range(2):
                p.dma(stg[:, :], ws2r[:, kh * 4:(kh + 1) * 4, :])
                p.cp(w2e[:, kh * 4:(kh + 1) * 4, :], stg4[:, :, :])
            p.dma(be1c[:, :], bs1.rearrange("(a p) -> p a", p=128), slow=True)
            p.dma(bstg[0:1, :], bs2[None, :])
            p.cp(be2rb[:, :], bstg[0:1, :])
            p.memset(g_rowb[:, :], 1.0)
            p.memset(G128[:, :], 1.0)
        for tb in range(4):
            c0, c1 = tb * 512, (tb + 1) * 512
            for m in range(8):
                for k in range(8):
                    p.mm(pg[:, :], w1e[:, k, m * 128:(m + 1) * 128],
                         x3b[:, k, c0:c1], start=(k == 0), stop=(k == 7),
                         wait=(k == 0), inc=(k == 7))
                p.act(hact[:, m, :], pg[:, :], AF.Gelu, bias=be1c[:, m:m + 1])
            for m in range(8):
                p.tt(hact[:, m, :], hact[:, m, :], G128[:, c0:c1], ALU.mult)
            for m in range(8):
                for k in range(8):
                    p.mm(pg[:, :], w2e[:, k, m * 128:(m + 1) * 128],
                         hact[:, k, :], start=(k == 0), stop=False,
                         wait=(k == 0), inc=False)
                p.mm(pg[:, :], be2rb[0:1, m * 128:(m + 1) * 128],
                     g_rowb[0:1, c0:c1], start=False, stop=True,
                     wait=False, inc=True)
                p.tt(acc[:, m, c0:c1], acc[:, m, c0:c1], pg[:, :], ALU.add)
    for tb in range(4):
        c0 = tb * 512
        for m in range(8):
            p.dma(x3s[:, :], x3rr[:, m, T0 + c0:T0 + c0 + 512])
            p.stt(acc[:, m, c0:c0 + 512], x3s[:, :], 2.0,
                  acc[:, m, c0:c0 + 512], ALU.mult, ALU.add)
        for m in range(8):
            p.mm(pab[0:1, 0:512], onesf[:, 0:1], acc[:, m, c0:c0 + 512],
                 start=(m == 0), stop=(m == 7), wait=(m == 0), inc=(m == 7))
        p.ts(mrow[:, :], pab[0:1, 0:512], 1.0 / D, ALU.mult)
        for m in range(8):
            p.act(sq[:, :], acc[:, m, c0:c0 + 512], AF.Square)
            p.mm(pab[0:1, 512:1024], onesf[:, 0:1], sq[:, :],
                 start=(m == 0), stop=(m == 7))
        p.ts(vrow[:, :], pab[0:1, 512:1024], 1.0 / D, ALU.mult)
        p.act(m2row[:, :], mrow[:, :], AF.Square)
        p.tt(vrow[:, :], vrow[:, :], m2row[:, :], ALU.subtract)
        p.act(vrow[:, :], vrow[:, :], AF.Ln, bias=cbias[0:1, 0:1])
        p.act(vrow[:, :], vrow[:, :], AF.Exp, scale=-0.5)
        p.tt(brow[:, :], mrow[:, :], vrow[:, :], ALU.mult)
        p.ts(brow[:, :], brow[:, :], -1.0, ALU.mult)
        p.mm(pab[:, 0:512], onesf[0:1, :], vrow[:, :])
        p.mm(pab[:, 512:1024], onesf[0:1, :], brow[:, :])
        for m in range(8):
            p.tt(acc[:, m, c0:c0 + 512], acc[:, m, c0:c0 + 512],
                 pab[:, 0:512], ALU.mult)
            p.tt(acc[:, m, c0:c0 + 512], acc[:, m, c0:c0 + 512],
                 pab[:, 512:1024], ALU.add)
        for j in range(4):
            for k in range(8):
                p.trans(ptr[:, :], acc[:, k, c0 + j * 128:c0 + (j + 1) * 128],
                        idf[:, :])
                p.cp(outc[:, k * 128:(k + 1) * 128], ptr[:, :])
            p.dma(outp[tb * 512 + j * 128:tb * 512 + (j + 1) * 128, :],
                  outc[:, :], eng="gpsimd")
    p.finalize(block)
    es.close()
    return nc


def _f32(x):
    return np.ascontiguousarray(np.asarray(x), dtype=None).astype(np.float32, copy=False)


def kernel(**inp):
    N = 8
    u = np.ascontiguousarray(np.asarray(inp["L0_attn_u"], dtype=np.float32))
    ident = np.eye(128, dtype=np.float32)
    tri = np.triu(np.ones((CH, CH), dtype=np.float32))
    sel = np.zeros((NH, NH * 128), dtype=np.float32)
    for h in range(NH):
        sel[h, h * 128:(h + 1) * 128] = 1.0
    e2 = np.zeros((NH, DIN), dtype=np.float32)
    for h in range(NH):
        e2[h, h * HD:(h + 1) * HD] = 1.0

    def A(name):
        return np.ascontiguousarray(np.asarray(inp[name], dtype=np.float32))

    w_in = A("L0_attn_W_in")
    conv_w = A("L0_attn_conv_w")
    conv_b = A("L0_attn_conv_b")
    dt_bias = A("L0_attn_dt_bias")
    a_log = A("L0_attn_A_log")
    norm_w = A("L0_attn_norm_w")
    w_out = A("L0_attn_W_out")
    w1t = np.ascontiguousarray(A("L0_ffn_W1").T)
    b1 = A("L0_ffn_b1")
    w2t = np.ascontiguousarray(A("L0_ffn_W2").T)
    b2 = A("L0_ffn_b2")
    scale = np.asarray(inp["L0_ffn_scale"], dtype=np.float32).reshape(())
    sc4 = np.ascontiguousarray(np.broadcast_to(scale, (4, 1))).astype(np.float32)
    wqk = A("L1_attn_Wqk")
    bqk = A("L1_attn_bqk")
    wv = A("L1_attn_Wv")
    bv = A("L1_attn_bv")
    ws1 = A("L1_ffn_ws1")
    bs1 = A("L1_ffn_bs1")
    ws2 = A("L1_ffn_ws2")
    bs2 = A("L1_ffn_bs2")
    we1 = A("L1_ffn_We1")
    be1 = A("L1_ffn_Be1")
    we2 = A("L1_ffn_We2")
    be2 = A("L1_ffn_Be2")
    wr = A("L1_ffn_Wr")
    br = A("L1_ffn_br")

    cores = list(range(N))
    nc = build_A1()
    maps = [{"u": u[i], "w_in": w_in, "w_out": w_out, "identb": ident} for i in range(N)]
    r1 = run_bass_kernel_spmd(nc, maps, core_ids=cores).results

    nc = build_A2()
    maps = [{"u": u[i], "zf": r1[i]["zf"], "xbcp": r1[i]["xbcp"],
             "dtraw": r1[i]["dtraw"], "woutb": r1[i]["woutb"],
             "conv_w": conv_w, "conv_b": conv_b, "dt_bias": dt_bias,
             "a_log": a_log, "norm_w": norm_w, "identf": ident,
             "tri": tri, "sel": sel, "e2": e2} for i in range(N)]
    r2 = run_bass_kernel_spmd(nc, maps, core_ids=cores).results

    nc = build_B()
    maps = [{"x1": r2[i]["x1"], "w1t": w1t, "b1": b1, "w2t": w2t, "b2": b2,
             "sc4": sc4, "identf": ident} for i in range(N)]
    r3 = run_bass_kernel_spmd(nc, maps, core_ids=cores).results

    nc = build_C()
    maps = [{"x2": r3[i]["x2"], "wqk": wqk, "bqk": bqk, "wv": wv, "bv": bv}
            for i in range(N)]
    r4 = run_bass_kernel_spmd(nc, maps, core_ids=cores).results

    outs = []
    for p_idx in range(2):
        nc = build_D(p_idx)
        maps = [{"x3": r4[i]["x3"], "ws1": ws1, "bs1": bs1, "ws2": ws2, "bs2": bs2,
                 "we1": we1, "be1": be1, "we2": we2, "be2": be2, "wr": wr, "br": br,
                 "identf": ident} for i in range(N)]
        outs.append(run_bass_kernel_spmd(nc, maps, core_ids=cores).results)

    res = np.empty((N, L, D), dtype=np.float32)
    for i in range(N):
        res[i, 0:2048] = outs[0][i]["outp"]
        res[i, 2048:4096] = outs[1][i]["outp"]
    return res
